# version 12
# speedup vs baseline: 1.2074x; 1.0532x over previous
import contextlib
import numpy as np

import concourse.bass as bass
import concourse.mybir as mybir
import concourse.bass_utils as _bass_utils
from concourse.bass_utils import run_bass_kernel_spmd

if not getattr(_bass_utils, "_ldwopt_patched", False):
    _orig_run_command = _bass_utils.run_command

    def _run_command_ldwopt(argv, **kw):
        argv = ["--enable-ldw-opt=true" if a == "--enable-ldw-opt=false" else a
                for a in argv]
        return _orig_run_command(argv, **kw)

    _bass_utils.run_command = _run_command_ldwopt
    _bass_utils._ldwopt_patched = True

F32 = mybir.dt.float32
BF16 = mybir.dt.bfloat16
AF = mybir.ActivationFunctionType
ALU = mybir.AluOpType
AX = mybir.AxisListType.X

NCORES = 8
T = 4096
NT = 32
D = 1024
KD = 8
NP = 2048
NE = 64
PS = 32
EPC = 8
F1 = 4096
LN_EPS = 1e-5
ATTN_EPS = 1e-6
GRP = 8
NGRP = NT // GRP

STAGE = 3
DBG_ATTN = False
DBG_FFN = False
DBG_INV = 0

NDMASEM = 8
SEMS = {
    "t": ("tensor", 1),
    "v": ("vector", 1),
    "s": ("scalar", 1),
    "g": ("gpsimd", 1),
    "cc": ("gpsimd", 1),
}
DMA_SEMS = []
for _q, _e in (("dsy", "sync"), ("dsc", "scalar"), ("dg", "gpsimd")):
    for _i in range(NDMASEM):
        SEMS[f"{_q}{_i}"] = (_e, 16)
        DMA_SEMS.append(f"{_q}{_i}")
DMA_SEMS = tuple(DMA_SEMS)
ENGS = ("tensor", "vector", "scalar", "sync", "gpsimd")


class Tok:
    __slots__ = ("sem", "idx", "used", "val", "hazard")
    def __init__(self, sem, idx, hazard):
        self.sem, self.idx, self.hazard = sem, idx, hazard
        self.used, self.val = False, None


class Prog:
    def __init__(self):
        self.ops = []
        self.per_sem = {k: [] for k in SEMS}
        self.semh = None
        self.phase = "init"
        self.nc = None

    def emit(self, eng, fn, waits=(), sem=None, hazard=False):
        ws = [w for w in waits if w is not None]
        for w in ws:
            w.used = True
        tok = None
        if sem is not None:
            tok = Tok(sem, len(self.per_sem[sem]), hazard)
            if sem in DMA_SEMS:
                tok.used = True
            self.per_sem[sem].append(tok)
        self.ops.append((eng, fn, ws, tok, self.phase))
        return tok

    def realize_values(self):
        for sem, toks in self.per_sem.items():
            v = 0
            for t in toks:
                if t.used:
                    v += SEMS[sem][1]
                    t.val = v

    def runner(self, eng_name):
        ops = [(f, w, tk, ph) for (e, f, w, tk, ph) in self.ops if e == eng_name]
        semh = self.semh
        nc = self.nc

        def run(eng):
            maxwait = {}
            cur = None
            sid = None
            for fn, waits, tok, ph in ops:
                if ph != cur:
                    if cur is not None:
                        nc.leave_named_scope(cur, sid, False)
                    sid, _ = nc.enter_named_scope(ph, False)
                    cur = ph
                for w in waits:
                    if w.val is None:
                        raise RuntimeError(f"unresolved token on {w.sem}")
                    if w.val <= maxwait.get(w.sem, 0):
                        continue
                    maxwait[w.sem] = w.val
                    eng.wait_ge(semh[w.sem], w.val)
                inst = fn(eng)
                if tok is not None and tok.used:
                    inst.then_inc(semh[tok.sem], SEMS[tok.sem][1])
            if cur is not None:
                nc.leave_named_scope(cur, sid, False)
        return run


def lastof(toks):
    best = {}
    for t in toks:
        if t is None:
            continue
        cur = best.get(t.sem)
        if cur is None or t.idx > cur.idx:
            best[t.sem] = t
    return list(best.values())


class Ring:
    def __init__(self, bufs, guards=()):
        self.bufs = bufs
        self.n = len(bufs)
        self.free = [list(guards) for _ in range(self.n)]

    def acquire(self, i):
        return self.bufs[i % self.n], list(self.free[i % self.n])

    def release(self, i, toks):
        if not isinstance(toks, (list, tuple)):
            toks = [toks]
        self.free[i % self.n] = [t for t in toks if t is not None]


class Mem:
    def __init__(self, nc, base, top):
        self.nc = nc
        self.segs = [(base, top, [])]
        self.ranges = {}

    def alloc(self, name, shape, dtype, align=64):
        import math
        esz = {F32: 4, BF16: 2}[dtype]
        nb = esz
        for s in shape[1:]:
            nb *= s
        nb = (nb + align - 1) // align * align
        for idx, (start, end, guards) in enumerate(self.segs):
            a = (start + align - 1) // align * align
            if end - a >= nb:
                h = self.nc.alloc_sbuf_tensor_at(name, shape, dtype, offset=a)
                rest = []
                if a > start:
                    rest.append((start, a, list(guards)))
                if a + nb < end:
                    rest.append((a + nb, end, list(guards)))
                self.segs[idx:idx + 1] = rest
                self.ranges[h.name] = (a, a + nb)
                return h, list(guards)
        raise RuntimeError(f"SBUF exhausted for {name} ({nb}B): "
                           f"{[(s, e) for s, e, _ in self.segs]}")

    def free(self, h, guards):
        a, b = self.ranges[h.name]
        gs = [g for g in guards if g is not None]
        self.segs.append((a, b, gs))
        self.segs.sort(key=lambda s: s[0])
        merged = []
        for s in self.segs:
            if merged and merged[-1][1] == s[0]:
                merged[-1] = (merged[-1][0], s[1], merged[-1][2] + s[2])
            else:
                merged.append(list(s))
        self.segs = [tuple(s) for s in merged]


def build_program(dbg=()):
    nc = bass.Bass("TRN2", target_bir_lowering=False, debug=False)
    P = Prog()
    P.nc = nc
    es = contextlib.ExitStack()

    def dram(name, shape, dtype):
        kind = "ExternalOutput" if name in dbg else "Internal"
        return nc.dram_tensor(name, shape, dtype, kind=kind)

    x0 = nc.dram_tensor("x0", [T, D], F32, kind="ExternalInput").ap()
    phiT = nc.dram_tensor("phiT", [D, NP], F32, kind="ExternalInput").ap()
    phi_b = nc.dram_tensor("phi_b", [1, NP], F32, kind="ExternalInput").ap()
    qkT = nc.dram_tensor("qkT", [D, 128], F32, kind="ExternalInput").ap()
    qk_b = nc.dram_tensor("qk_b", [1, 128], F32, kind="ExternalInput").ap()
    vT = nc.dram_tensor("vT", [D, D], F32, kind="ExternalInput").ap()
    v_b = nc.dram_tensor("v_b", [1, D], F32, kind="ExternalInput").ap()
    oT = nc.dram_tensor("oT", [D, D], F32, kind="ExternalInput").ap()
    o_b = nc.dram_tensor("o_b", [1, D], F32, kind="ExternalInput").ap()
    f1T = nc.dram_tensor("f1T", [D, F1], F32, kind="ExternalInput").ap()
    f1_b = nc.dram_tensor("f1_b", [1, F1], F32, kind="ExternalInput").ap()
    f2T = nc.dram_tensor("f2T", [F1, D], F32, kind="ExternalInput").ap()
    f2_b = nc.dram_tensor("f2_b", [1, D], F32, kind="ExternalInput").ap()
    ln1w = nc.dram_tensor("ln1w", [128, KD], F32, kind="ExternalInput").ap()
    ln1b = nc.dram_tensor("ln1b", [1, D], F32, kind="ExternalInput").ap()
    ln2w = nc.dram_tensor("ln2w", [128, KD], F32, kind="ExternalInput").ap()
    ln2b = nc.dram_tensor("ln2b", [1, D], F32, kind="ExternalInput").ap()
    ew = nc.dram_tensor("ew", [EPC, D, D], F32, kind="ExternalInput").ap()
    eb = nc.dram_tensor("eb", [1, EPC * D], F32, kind="ExternalInput").ap()
    ln1b_col = nc.dram_tensor("ln1b_col", [128, KD], F32, kind="ExternalInput").ap()
    ln2b_col = nc.dram_tensor("ln2b_col", [128, KD], F32, kind="ExternalInput").ap()
    f1b_col = nc.dram_tensor("f1b_col", [128, 32], F32, kind="ExternalInput").ap()
    pmask = nc.dram_tensor("pmask", [64, NCORES], F32, kind="ExternalInput").ap()

    x1_dram = dram("x1_dram", [T, D], F32)
    xmid_dram = dram("xmid_dram", [T, D], F32)
    x2_dram = dram("x2_dram", [T, D], F32)
    ct_dram = dram("ct_dram", [NT, 16, 128, 128], BF16)
    kv_dram = dram("kv_dram", [NT, 64, D], BF16)
    if DBG_FFN:
        d_xp = nc.dram_tensor("d_xp", [NP, D], F32, kind="ExternalOutput").ap()
        d_ct = nc.dram_tensor("d_ct", [16, 128, 128], BF16,
                              kind="ExternalOutput").ap()
        d_xn = nc.dram_tensor("d_xn", [128, D], BF16, kind="ExternalOutput").ap()
        d_E = nc.dram_tensor("d_E", [128, NP], BF16, kind="ExternalOutput").ap()
        d_yt = nc.dram_tensor("d_yt", [128, 16 * D], BF16,
                              kind="ExternalOutput").ap()
        d_stats = nc.dram_tensor("d_stats", [128, 11 * NT], F32,
                                 kind="ExternalOutput").ap()
    rs_in = dram("rs_in", [NP, D], F32)
    rs_out = dram("rs_out", [NP // NCORES, D], F32)
    ag_in = dram("ag_in", [NP // NCORES, D], BF16)
    ag_out = dram("ag_out", [NP, D], BF16)
    if DBG_ATTN:
        d_q = nc.dram_tensor("d_q", [64, T], F32, kind="ExternalOutput").ap()
        d_k = nc.dram_tensor("d_k", [64, T], F32, kind="ExternalOutput").ap()
        d_v = nc.dram_tensor("d_v", [128, D], F32, kind="ExternalOutput").ap()
        d_am = nc.dram_tensor("d_am", [128, 128], F32, kind="ExternalOutput").ap()
        d_den = nc.dram_tensor("d_den", [1, 256], F32, kind="ExternalOutput").ap()
        d_at = nc.dram_tensor("d_at", [128, KD * 128], F32,
                              kind="ExternalOutput").ap()
        d_qkb = nc.dram_tensor("d_qkb", [1, 128], F32, kind="ExternalOutput").ap()
        d_vb = nc.dram_tensor("d_vb", [1, D], F32, kind="ExternalOutput").ap()
        d_xin = nc.dram_tensor("d_xin", [128, D], F32, kind="ExternalOutput").ap()
        d_dr = nc.dram_tensor("d_dr", [128, 128], F32, kind="ExternalOutput").ap()
        d_stall = nc.dram_tensor("d_stall", [64, NCORES * 1026], F32,
                                 kind="ExternalOutput").ap()
        d_kv = nc.dram_tensor("d_kv", [64, 1024], F32,
                              kind="ExternalOutput").ap()
    st_ag_in = dram("st_ag_in", [64, 1026], F32)
    st_ag_out = dram("st_ag_out", [NCORES * 64, 1026], F32)
    yout = nc.dram_tensor("y", [T, D], F32, kind="ExternalOutput").ap()

    mem = Mem(nc, 16512, 229344)

    def salloc(name, shape, dtype):
        h, guards = mem.alloc(name, shape, dtype)
        assert not guards, f"{name} allocated over guarded space"
        return h

    ident = salloc("ident", [128, 128], BF16)
    maskT = salloc("maskT", [128, 128], F32)
    ones_bf = salloc("ones_bf", [1, 128], BF16)
    ones_f = salloc("ones_f", [1, 128], F32)
    onecol_bf = salloc("onecol_bf", [128, 1], BF16)

    stats = salloc("stats", [128, 11 * NT], F32)
    SC_SSQ, SC_NSUM, SC_SCALE, SC_NMR = 0, 1, 2, 3
    SC_SCALE1, SC_NMR1, SC_SCALE2, SC_NMR2 = 4, 5, 6, 7
    SC_SC12, SC_NM12, SC_TMP = 8, 9, 10
    sm = salloc("sm", [128, 256], F32)
    sm_bf = salloc("sm_bf", [128, 128], BF16)
    sq_scr = salloc("sq_scr", [128, D], F32)
    v_scr = salloc("v_scr", [128, 32], F32)

    xbuf = Ring([salloc(f"xbuf{i}", [128, D], F32) for i in range(2)])
    rbuf = Ring([salloc(f"rb{i}", [128, D], F32) for i in range(2)])

    pA = nc.alloc_psum_tensor("pA", [128, 1024], F32)
    pB = nc.alloc_psum_tensor("pB", [128, 1024], F32)
    pX = nc.alloc_psum_tensor("pX", [128, 1024], F32)
    pT = nc.alloc_psum_tensor("pT", [128, 1024], BF16)
    pC = nc.alloc_psum_tensor("pC", [128, 1024], BF16)

    P.semh = {s: es.enter_context(nc.semaphore(s)) for s in SEMS}

    dma_rr = {"sync": 0, "scalar": 0, "gpsimd": 0}

    def dma(queue, out, in_, waits=()):
        prefix = {"sync": "dsy", "scalar": "dsc", "gpsimd": "dg"}[queue]
        i = dma_rr[queue]
        dma_rr[queue] = i + 1
        semname = f"{prefix}{i % NDMASEM}"
        prev = P.per_sem[semname][-1] if P.per_sem[semname] else None
        w = list(waits) + ([prev] if prev is not None else [])
        return P.emit(queue, lambda e, o=out, i_=in_: e.dma_start(out=o, in_=i_),
                      waits=w, sem=semname)

    def t_op(fn, waits=(), tok=True):
        return P.emit("tensor", fn, waits=waits, sem="t" if tok else None)

    def v_op(fn, waits=(), tok=True, hazard=False):
        return P.emit("vector", fn, waits=waits, sem="v" if tok else None,
                      hazard=hazard)

    def s_op(fn, waits=(), tok=True, hazard=False):
        return P.emit("scalar", fn, waits=waits, sem="s" if tok else None,
                      hazard=hazard)

    def g_op(fn, waits=()):
        return P.emit("gpsimd", fn, waits=waits, sem="g", hazard=True)

    def coll(kind, op, ins, outs, waits):
        return P.emit("gpsimd", lambda e: e.collective_compute(
            kind, op, replica_groups=[list(range(NCORES))], ins=ins, outs=outs),
            waits=waits, sem="cc", hazard=True)

    def ln_stats(xb_ap, c, sc_scale, sc_nmr, waits, eps=LN_EPS, par=0):
        ssq = stats[:, SC_SSQ * NT + c: SC_SSQ * NT + c + 1]
        nsum = stats[:, SC_NSUM * NT + c: SC_NSUM * NT + c + 1]
        scale = stats[:, sc_scale * NT + c: sc_scale * NT + c + 1]
        nmr = stats[:, sc_nmr * NT + c: sc_nmr * NT + c + 1]
        vs = v_scr[:, par * 8: par * 8 + 8]
        n = xb_ap.shape[-1]
        tok_s = s_op(lambda e: e.activation(out=sq_scr[:, 0:n], in_=xb_ap,
                                            func=AF.Square, accum_out=ssq),
                     waits=waits, hazard=True)
        tok_v = v_op(lambda e: e.reduce_sum(nsum, xb_ap, AX, negate=True),
                     waits=waits, hazard=True)

        def smalls(e):
            e.tensor_scalar_mul(nmr, nsum, 1.0 / n)
            e.tensor_tensor(vs[:, 0:1], nmr, nmr, ALU.mult)
            e.tensor_scalar(vs[:, 1:2], ssq, 1.0 / n, eps, ALU.mult, ALU.add)
            e.tensor_sub(vs[:, 1:2], vs[:, 1:2], vs[:, 0:1])
            return e.reciprocal(vs[:, 2:3], vs[:, 1:2])
        tok1 = v_op(smalls, waits=[tok_s, tok_v], hazard=True)
        tok2 = s_op(lambda e: e.activation(out=scale, in_=vs[:, 2:3], func=AF.Sqrt),
                    waits=[tok1], hazard=True)
        tok3 = v_op(lambda e: e.tensor_mul(nmr, nmr, scale), waits=[tok2],
                    hazard=True)
        return tok3

    def ln_apply(dst, xb_ap, c, sc_scale, sc_nmr, waits):
        scale = stats[:, sc_scale * NT + c: sc_scale * NT + c + 1]
        nmr = stats[:, sc_nmr * NT + c: sc_nmr * NT + c + 1]
        return s_op(lambda e: e.activation(out=dst, in_=xb_ap, func=AF.Identity,
                                           bias=nmr, scale=scale), waits=waits)

    sq_chain = [None]

    def raw_stats(xb_ap, c, waits):
        ssq = stats[:, SC_SSQ * NT + c: SC_SSQ * NT + c + 1]
        nsum = stats[:, SC_NSUM * NT + c: SC_NSUM * NT + c + 1]
        n = xb_ap.shape[-1]
        w = list(waits) + ([sq_chain[0]] if sq_chain[0] is not None else [])
        ts_ = s_op(lambda e: e.activation(out=sq_scr[:, 0:n], in_=xb_ap,
                                          func=AF.Square, accum_out=ssq),
                   waits=w, hazard=True)
        sq_chain[0] = ts_
        tv_ = v_op(lambda e: e.reduce_sum(nsum, xb_ap, AX, negate=True),
                   waits=waits, hazard=True)
        return ts_, tv_

    def stats_finalize(n, sc_scale, sc_nmr, waits, nelem=D, eps=LN_EPS):
        SSQ = stats[:, SC_SSQ * NT: SC_SSQ * NT + n]
        NSUM = stats[:, SC_NSUM * NT: SC_NSUM * NT + n]
        SCALE = stats[:, sc_scale * NT: sc_scale * NT + n]
        NMR = stats[:, sc_nmr * NT: sc_nmr * NT + n]

        TMP = stats[:, SC_TMP * NT: SC_TMP * NT + n]
        t = v_op(lambda e: e.tensor_scalar_mul(NMR, NSUM, 1.0 / nelem),
                 waits=waits, hazard=True)
        t = v_op(lambda e: e.tensor_tensor(SCALE, NMR, NMR, ALU.mult),
                 waits=[t], hazard=True)
        t0 = v_op(lambda e: e.tensor_scalar(TMP, SSQ, 1.0 / nelem, eps,
                                            ALU.mult, ALU.add),
                  waits=waits, hazard=True)
        t = v_op(lambda e: e.tensor_sub(SSQ, TMP, SCALE), waits=[t, t0],
                 hazard=True)
        t1 = v_op(lambda e: e.reciprocal(NSUM, SSQ), waits=[t], hazard=True)
        t2 = s_op(lambda e: e.activation(out=SCALE, in_=NSUM, func=AF.Sqrt),
                  waits=[t1], hazard=True)
        t3 = v_op(lambda e: e.tensor_mul(TMP, NMR, SCALE), waits=[t2],
                  hazard=True)
        t3 = v_op(lambda e: e.tensor_copy(NMR, TMP), waits=[t3], hazard=True)
        return t3

    tr_state = {}

    def transpose_tiles(src_fn, n, psum_bf, dst_fn, waits, evac_eng="scalar",
                        dst_guard=None, banks=None):
        if banks is None:
            banks = [psum_bf]
        mt = None
        ev = None
        for bi, base in enumerate(range(0, n, 8)):
            pb = banks[bi % len(banks)]
            key = pb.name
            cnt = min(8, n - base)
            w = list(waits) + lastof([tr_state.get(key)])
            for j in range(cnt):
                mt = t_op(lambda e, j=j, base=base, pb=pb: e.transpose(
                    pb[:, j * 128:(j + 1) * 128], src_fn(base + j), ident[:]),
                    waits=w if j == 0 else (), tok=(j == cnt - 1))
            evw = [mt]
            if dst_guard and base == 0:
                evw += dst_guard if isinstance(dst_guard, list) else [dst_guard]
            if evac_eng == "scalar":
                ev = s_op(lambda e, base=base, cnt=cnt, pb=pb: e.copy(
                    dst_fn(base, cnt), pb[:, 0:cnt * 128]), waits=evw)
            else:
                ev = v_op(lambda e, base=base, cnt=cnt, pb=pb: e.tensor_copy(
                    dst_fn(base, cnt), pb[:, 0:cnt * 128]), waits=evw)
            tr_state[key] = ev
        return mt, ev
    t_ = g_op(lambda e: e.memset(ident[:], 1.0))
    t_ = g_op(lambda e: e.affine_select(
        ident[:], ident[:], [[1, 128]], ALU.is_equal, 0.0,
        channel_multiplier=-1), waits=[t_])
    t2_ = g_op(lambda e: e.memset(maskT[:], 1.0), waits=[t_])
    t2_ = g_op(lambda e: e.affine_select(
        maskT[:], maskT[:], [[1, 128]], ALU.is_ge, 0.0,
        channel_multiplier=-1), waits=[t2_])
    t3_ = g_op(lambda e: e.memset(ones_bf[:], 1.0), waits=[t2_])
    t3_ = g_op(lambda e: e.memset(ones_f[:], 1.0), waits=[t3_])
    tok_const = g_op(lambda e: e.memset(onecol_bf[:], 1.0), waits=[t3_])
    def ffn(inv, xin, res_dram, have_stats, stats_dst, final,
            stats_ready=None, pg=None, xsrc=None):
        P.phase = f"F{inv}A"
        sfx = f"_{inv}"
        phiT_sb, gu = mem.alloc("phiT_sb" + sfx, [128, KD * NP], BF16)
        tok_phi = dma("gpsimd", phiT_sb[:].rearrange("p (k n) -> p k n", n=NP),
                      phiT.rearrange("(k p) n -> p k n", p=128), waits=gu)
        phib_sb, gu = mem.alloc("phib_sb" + sfx, [1, NP], BF16)
        tok_phib = dma("gpsimd", phib_sb[:], phi_b, waits=gu)
        xnT_hs = [mem.alloc(f"xnT{i}" + sfx, [128, KD * 128], BF16)
                  for i in range(2)]
        xnT_sb = Ring([h for h, _ in xnT_hs],
                      guards=lastof([g for _, gu in xnT_hs for g in gu]))
        E_hs = [mem.alloc(f"E{i}" + sfx, [128, NP], BF16) for i in range(2)]
        E_sb = Ring([h for h, _ in E_hs],
                    guards=lastof([g for _, gu in E_hs for g in gu]))
        ctt_hs = [mem.alloc(f"ctt{i}" + sfx, [128, NP], BF16) for i in range(2)]
        ctt_sb = Ring([h for h, _ in ctt_hs],
                      guards=lastof([g for _, gu in ctt_hs for g in gu]))
        ctT_hs = [mem.alloc(f"ctT{i}" + sfx, [128, 16 * 128], BF16)
                  for i in range(2)]
        ctT_sb = Ring([h for h, _ in ctT_hs],
                      guards=lastof([g for _, gu in ctT_hs for g in gu]))
        xn_grp, xn_gu = mem.alloc("xn_grp" + sfx, [128, GRP * D], BF16)
        D_grp, D_gu = mem.alloc("D_grp" + sfx, [128, GRP * NP], BF16)
        X_acc, X_gu = mem.alloc("X_acc" + sfx, [128, 16 * D], F32)

        if have_stats is None:
            rs_toks = []
            for c in range(NT):
                xb, xw = xbuf.acquire(c)
                xtok = dma("sync", xb[:], xin[c * 128:(c + 1) * 128, :],
                           waits=xw)
                ts_, tv_ = raw_stats(xb[:], c, [xtok])
                xbuf.release(c, [ts_, tv_])
                rs_toks = [ts_, tv_]
            stats_tok0 = stats_finalize(NT, SC_SCALE, SC_NMR, rs_toks)
            sc_sc, sc_nm = SC_SCALE, SC_NMR
        else:
            stats_tok0 = stats_ready
            sc_sc, sc_nm = have_stats

        prev_xmm = {"tok": None}
        exp_guard = [None, None]
        ct_all = []
        flush_toks = [None] * 16
        last_tr = {"xnT": None}
        fe_out = {}
        dtoks = {}
        ct_pend = {}

        def be2(c):
            ctt, ctok = ct_pend.pop(c)
            ctb, ctbw = ctT_sb.acquire(c)
            trt, ev_ct = transpose_tiles(
                lambda j, ctt=ctt: ctt[:, j * 128:(j + 1) * 128],
                16, pC, lambda b, n, ctb=ctb: ctb[:, b * 128:(b + n) * 128],
                [ctok], evac_eng="scalar", dst_guard=ctbw, banks=[pC, pT])
            ctt_sb.release(c, trt)
            sd = dma("sync", ct_dram.ap()[c].rearrange("j q t -> q j t"),
                     ctb[:].rearrange("q (j t) -> q j t", t=128),
                     waits=[ev_ct])
            ctT_sb.release(c, sd)
            ct_all.append(sd)

        def frontend(c):
            xb, xw = xbuf.acquire(c)
            xtok = dma("scalar", xb[:], xin[c * 128:(c + 1) * 128, :],
                       waits=xw + ([xsrc[c]] if xsrc else []))
            ci = c % GRP
            xn_sl = xn_grp[:, ci * D:(ci + 1) * D]
            aw = [stats_tok0, xtok]
            if c == 0:
                aw += xn_gu
            if c % GRP == 0 and c > 0:
                aw.append(prev_xmm["tok"])
            atok = ln_apply(xn_sl, xb[:], c, sc_sc, sc_nm, aw)
            xbuf.release(c, atok)
            xT, xTw = xnT_sb.acquire(c)
            trm, ev_xnT = transpose_tiles(
                lambda j, xn_sl=xn_sl: xn_sl[:, j * 128:(j + 1) * 128],
                KD, pT, lambda b, n, xT=xT: xT[:, b * 128:(b + n) * 128],
                [atok, tok_phi, tok_phib, tok_const], evac_eng="scalar",
                dst_guard=xTw)
            last_tr["xnT"] = trm
            fe_out[c] = (xT, ev_xnT, xn_sl)

        def backend(c):
            ci = c % GRP
            xT, ev_xnT, xn_sl = fe_out.pop(c)
            Eb, Ew = E_sb.acquire(c)
            mt = None
            for h in range(2):
                pb = pA if h == 0 else pB
                w0 = [ev_xnT]
                if exp_guard[h] is not None:
                    w0.append(exp_guard[h])
                elif pg is not None:
                    w0.append(pg["pA" if h == 0 else "pB"])
                t_op(lambda e, h=h, pb=pb: e.matmul(
                    pb[:, 0:512], ones_bf[:],
                    phib_sb[:, h * 1024: h * 1024 + 512],
                    start=True, stop=False), waits=w0, tok=False)
                t_op(lambda e, h=h, pb=pb: e.matmul(
                    pb[:, 512:1024], ones_bf[:],
                    phib_sb[:, h * 1024 + 512: (h + 1) * 1024],
                    start=True, stop=False), tok=False)
                for k in range(KD):
                    t_op(lambda e, k=k, h=h, pb=pb, xT=xT: e.matmul(
                        pb[:, 0:512], xT[:, k * 128:(k + 1) * 128],
                        phiT_sb[:, k * NP + h * 1024: k * NP + h * 1024 + 512],
                        start=False, stop=(k == KD - 1)), tok=False)
                    mt = t_op(lambda e, k=k, h=h, pb=pb, xT=xT: e.matmul(
                        pb[:, 512:1024], xT[:, k * 128:(k + 1) * 128],
                        phiT_sb[:, k * NP + h * 1024 + 512: k * NP + (h + 1) * 1024],
                        start=False, stop=(k == KD - 1)), tok=(k == KD - 1))
                ew_ = [mt]
                if h == 0 and Ew:
                    ew_ += Ew
                exp_guard[h] = s_op(lambda e, h=h, pb=pb, Eb=Eb: e.activation(
                    out=Eb[:, h * 1024:(h + 1) * 1024], in_=pb[:],
                    func=AF.Exp), waits=ew_)
            xnT_sb.release(c, mt)
            etok0, etok1 = exp_guard[0], exp_guard[1]
            spv = sm[:, 0:64]
            sptok = v_op(lambda e, Eb=Eb: e.reduce_sum(
                spv, Eb[:].rearrange("p (n q) -> p n q", q=PS), AX),
                waits=[etok0, etok1], hazard=True)
            spit = v_op(lambda e: e.reciprocal(sm[:, 64:128], spv),
                        waits=[sptok], hazard=True)
            spbt = v_op(lambda e: e.tensor_copy(sm_bf[:, 0:64], sm[:, 64:128]),
                        waits=[spit], hazard=True)
            dsl = D_grp[:, ci * NP:(ci + 1) * NP]
            dw = [spbt]
            if c == 0:
                dw += D_gu
            if ci == 0 and c > 0:
                dw.append(prev_xmm["tok"])
            dtok = v_op(lambda e, Eb=Eb, dsl=dsl: e.tensor_tensor(
                dsl.rearrange("p (n q) -> p n q", q=PS),
                Eb[:].rearrange("p (n q) -> p n q", q=PS),
                sm_bf[:, 0:64].unsqueeze(2).broadcast_to([128, NE, PS]),
                ALU.mult), waits=dw)
            snv = sm[:, 128:160]
            sntok = v_op(lambda e, Eb=Eb: e.reduce_sum(
                snv, Eb[:].rearrange("p (n q) -> p q n", q=PS), AX),
                waits=[etok0, etok1], hazard=True)
            snit = v_op(lambda e: e.reciprocal(sm[:, 160:192], snv),
                        waits=[sntok], hazard=True)
            snbt = v_op(lambda e: e.tensor_scalar_mul(
                sm_bf[:, 64:96], sm[:, 160:192], 0.5), waits=[snit],
                hazard=True)
            ctt, cttw = ctt_sb.acquire(c)
            ctok = v_op(lambda e, Eb=Eb, ctt=ctt: e.tensor_tensor(
                ctt[:].rearrange("p (n q) -> p n q", q=PS),
                Eb[:].rearrange("p (n q) -> p n q", q=PS),
                sm_bf[:, 64:96].unsqueeze(1).broadcast_to([128, NE, PS]),
                ALU.mult), waits=[snbt] + cttw)
            E_sb.release(c, ctok)
            ct_pend[c] = (ctt, ctok)
            dtoks[c] = dtok
            if DBG_FFN and inv == DBG_INV and c == 0:
                dma("gpsimd", d_stats, stats[:], waits=[dtok])
                dma("gpsimd", d_xn, xn_sl, waits=[dtok])
                dma("gpsimd", d_E, Eb[:], waits=[ctok])

        def xmms(g):
            gd = dtoks[g * GRP + GRP - 1]
            xmm = None
            ft = None
            for j in range(16):
                w0 = [gd]
                if flush_toks[j] is not None:
                    w0.append(flush_toks[j])
                elif pg is not None and j == 0:
                    w0.append(pg["pX"])
                if ft is not None:
                    w0.append(ft)
                for ti in range(GRP):
                    for h in range(2):
                        xmm = t_op(lambda e, ti=ti, j=j, h=h: e.matmul(
                            pX[:, h * 512:(h + 1) * 512],
                            D_grp[:, ti * NP + j * 128: ti * NP + (j + 1) * 128],
                            xn_grp[:, ti * D + h * 512: ti * D + (h + 1) * 512],
                            start=(ti == 0), stop=(ti == GRP - 1)),
                            waits=w0 if (ti == 0 and h == 0) else (),
                            tok=(ti == GRP - 1 and h == 1))
                if g == 0:
                    ft = v_op(lambda e, j=j: e.tensor_copy(
                        X_acc[:, j * D:(j + 1) * D], pX[:]),
                        waits=[xmm] + (X_gu if j == 0 else []))
                else:
                    ft = v_op(lambda e, j=j: e.tensor_add(
                        X_acc[:, j * D:(j + 1) * D],
                        X_acc[:, j * D:(j + 1) * D], pX[:]), waits=[xmm])
                flush_toks[j] = ft
            prev_xmm["tok"] = xmm

        n_fe = 0
        for c in range(NT):
            while (n_fe <= c + 1 and n_fe < NT and
                   not (n_fe % GRP == 0 and n_fe > c)):
                frontend(n_fe)
                n_fe += 1
            backend(c)
            if c >= 1:
                be2(c - 1)
            if (c + 1) % GRP == 0:
                xmms(c // GRP)
        be2(NT - 1)

        if DBG_FFN and inv == DBG_INV:
            dma("gpsimd", d_xp.rearrange("(j p) d -> p j d", p=128),
                X_acc[:].rearrange("p (j d) -> p j d", d=D),
                waits=lastof(flush_toks))
            dma("gpsimd", d_ct, ct_dram.ap()[0], waits=[ct_all[0]])
        P.phase = f"F{inv}B"
        tok_rsin = dma("sync", rs_in.ap().rearrange("(j p) d -> p j d", p=128),
                       X_acc[:].rearrange("p (j d) -> p j d", d=D),
                       waits=lastof(flush_toks))
        tok_rs = coll("ReduceScatter", ALU.add, [rs_in.ap().opt()],
                      [rs_out.ap().opt()], [tok_rsin])
        mem.free(phiT_sb, [prev_xmm["tok"]])
        mem.free(phib_sb, [prev_xmm["tok"]])
        for (h, _) in xnT_hs:
            mem.free(h, [prev_xmm["tok"]])
        for i, (h, _) in enumerate(E_hs):
            mem.free(h, E_sb.free[i])
        for i, (h, _) in enumerate(ctt_hs):
            mem.free(h, ctt_sb.free[i])
        for i, (h, _) in enumerate(ctT_hs):
            mem.free(h, ctT_sb.free[i])
        mem.free(xn_grp, [prev_xmm["tok"]])
        mem.free(D_grp, [prev_xmm["tok"]])
        mem.free(X_acc, [tok_rsin])
        Xs_sb, gu_xs = mem.alloc("Xs_sb" + sfx, [128, 2 * D], F32)
        Xt_bf, gu_xt = mem.alloc("Xt_bf" + sfx, [128, 2 * D], BF16)
        XT_sb, gu_XT = mem.alloc("XT_sb" + sfx, [128, 16 * 128], BF16)
        ew_hs = [mem.alloc(f"ew_sb{i}" + sfx, [128, KD * D], BF16)
                 for i in range(2)]
        ew_sb = Ring([h for h, _ in ew_hs])
        ewb_sb, gu_ewb = mem.alloc("ewb_sb" + sfx, [1, EPC * D], BF16)
        Y_sb, gu_y = mem.alloc("Y_sb" + sfx, [128, 2 * D], BF16)
        Yt_sb, gu_yt = mem.alloc("Yt_sb" + sfx, [128, 16 * D], BF16)

        ew_toks = [None] * EPC
        for e_i in range(2):
            b = ew_sb.bufs[e_i]
            ew_toks[e_i] = dma("gpsimd", b[:].rearrange("p (k h) -> p k h", h=D),
                               ew[e_i].rearrange("(k p) h -> p k h", p=128),
                               waits=ew_hs[e_i][1])
        tok_ewb = dma("gpsimd", ewb_sb[:], eb, waits=gu_ewb)
        Xc_sb, gu_xc = mem.alloc("Xc_sb" + sfx, [128, 2 * D], F32)
        tok_xs = dma("sync", Xc_sb[:].rearrange("p (i d) -> p i d", d=D),
                     rs_out.ap().rearrange("(i p) d -> p i d", p=128),
                     waits=[tok_rs] + gu_xc)
        tok_clip = v_op(lambda e: e.tensor_scalar(
            Xs_sb[:], Xc_sb[:], -33000.0, 65000.0, ALU.max, ALU.min),
            waits=[tok_xs] + gu_xs, hazard=True)
        rs2 = []
        for i in range(2):
            a_, b_ = raw_stats(Xs_sb[:, i * D:(i + 1) * D], i, [tok_clip])
            rs2 += [a_, b_]
        stats_tokB = stats_finalize(2, SC_SCALE, SC_NMR, rs2)
        tanh_toks = []
        for i in range(2):
            sl = Xs_sb[:, i * D:(i + 1) * D]
            aw = [stats_tokB, tok_clip]
            if sq_chain[0] is not None:
                aw.append(sq_chain[0])
            if tanh_toks:
                aw.append(tanh_toks[-1])
            at = ln_apply(sq_scr[:], sl, i, SC_SCALE, SC_NMR, aw)
            tt = s_op(lambda e, i=i: e.activation(
                out=Xt_bf[:, i * D:(i + 1) * D], in_=sq_scr[:], func=AF.Tanh),
                waits=[at] + (gu_xt if i == 0 else []), hazard=True)
            tanh_toks.append(tt)
            sq_chain[0] = tt
        ev_xt = None
        for i in range(2):
            _, ev_xt = transpose_tiles(
                lambda j, i=i: Xt_bf[:, i * D + j * 128: i * D + (j + 1) * 128],
                KD, pT, lambda b, n, i=i: XT_sb[:, (i * 8 + b) * 128:(i * 8 + b + n) * 128],
                [tanh_toks[i], last_tr["xnT"]], evac_eng="scalar",
                dst_guard=gu_XT if i == 0 else None)
        y_evac = None
        mt = None
        for e_i in range(EPC):
            i, off = e_i // 4, (e_i % 4) * 32
            ebuf = ew_sb.bufs[e_i % 2]
            w0 = [ew_toks[e_i], ev_xt, tok_ewb, exp_guard[0]]
            if y_evac is not None:
                w0.append(y_evac)
            t_op(lambda e, e_i=e_i: e.matmul(
                pA[0:32, 0:512], ones_bf[:, 0:32],
                ewb_sb[:, e_i * D: e_i * D + 512], start=True, stop=False),
                waits=w0, tok=False)
            t_op(lambda e, e_i=e_i: e.matmul(
                pA[0:32, 512:1024], ones_bf[:, 0:32],
                ewb_sb[:, e_i * D + 512:(e_i + 1) * D], start=True, stop=False),
                tok=False)
            for k in range(KD):
                for h in range(2):
                    mt = t_op(lambda e, i=i, k=k, off=off, h=h, ebuf=ebuf: e.matmul(
                        pA[0:32, h * 512:(h + 1) * 512],
                        XT_sb[:, (i * 8 + k) * 128 + off: (i * 8 + k) * 128 + off + 32],
                        ebuf[:, k * D + h * 512: k * D + (h + 1) * 512],
                        start=False, stop=(k == KD - 1)),
                        tok=(k == KD - 1 and h == 1))
            if e_i + 2 < EPC:
                ew_toks[e_i + 2] = dma(
                    "gpsimd", ebuf[:].rearrange("p (k h) -> p k h", h=D),
                    ew[e_i + 2].rearrange("(k p) h -> p k h", p=128),
                    waits=[mt])
            y_evac = v_op(lambda e, i=i, off=off: e.tensor_copy(
                Y_sb[off:off + 32, i * D:(i + 1) * D], pA[0:32, :]),
                waits=[mt] + (gu_y if e_i == 0 else []))
        tok_agin = dma("sync", ag_in.ap().rearrange("(i p) h -> p i h", p=128),
                       Y_sb[:].rearrange("p (i h) -> p i h", h=D),
                       waits=[y_evac])
        tok_ag = coll("AllGather", ALU.bypass, [ag_in.ap().opt()],
                      [ag_out.ap().opt()], [tok_agin])
        tok_yt = dma("sync", Yt_sb[:].rearrange("p (j h) -> p j h", h=D),
                     ag_out.ap().rearrange("(j p) h -> p j h", p=128),
                     waits=[tok_ag] + gu_yt)
        if DBG_FFN and inv == DBG_INV:
            dma("gpsimd", d_yt, Yt_sb[:], waits=[tok_yt])
        mem.free(Xs_sb, tanh_toks)
        mem.free(Xc_sb, [tok_clip])
        mem.free(Xt_bf, [ev_xt])
        mem.free(XT_sb, [mt])
        for i, (h, _) in enumerate(ew_hs):
            mem.free(h, [mt])
        mem.free(ewb_sb, [mt])
        mem.free(Y_sb, [tok_agin])

        P.phase = f"F{inv}C"
        ct_hs = [mem.alloc(f"ct_in{i}" + sfx, [128, 16 * 128], BF16)
                 for i in range(2)]
        ct_in = Ring([h for h, _ in ct_hs])
        res_toks = []
        rstat_toks = []
        ld_tok = [None, None]
        evac_prev = [None, None]
        for c in range(2):
            buf = ct_in.bufs[c]
            ld_tok[c] = dma("sync", buf[:].rearrange("q (j t) -> q j t", t=128),
                            ct_dram.ap()[c].rearrange("j q t -> q j t"),
                            waits=[ct_all[c]] + ct_hs[c][1])
        fc_mt = None
        for c in range(NT):
            ctbuf = ct_in.bufs[c % 2]
            pb = pA if c % 2 == 0 else pB
            w0 = [ld_tok[c % 2], tok_yt]
            if c < 2:
                w0 += [y_evac, exp_guard[1]]
            if evac_prev[c % 2] is not None:
                w0.append(evac_prev[c % 2])
            mt = None
            for j in range(16):
                for h in range(2):
                    mt = t_op(lambda e, j=j, h=h, pb=pb, ctbuf=ctbuf: e.matmul(
                        pb[:, h * 512:(h + 1) * 512],
                        ctbuf[:, j * 128:(j + 1) * 128],
                        Yt_sb[:, j * D + h * 512: j * D + (h + 1) * 512],
                        start=(j == 0), stop=(j == 15)),
                        waits=w0 if (j == 0 and h == 0) else (),
                        tok=(j == 15 and h == 1))
            fc_mt = mt
            if c + 2 < NT:
                ld_tok[c % 2] = dma("sync",
                                    ctbuf[:].rearrange("q (j t) -> q j t", t=128),
                                    ct_dram.ap()[c + 2].rearrange("j q t -> q j t"),
                                    waits=[mt])
            xb, xw = xbuf.acquire(c)
            xtok = dma("scalar", xb[:], xin[c * 128:(c + 1) * 128, :],
                       waits=xw + ([xsrc[c]] if xsrc else []))
            rb, rw = rbuf.acquire(c)
            ev = v_op(lambda e, pb=pb, xb=xb, rb=rb: e.tensor_add(
                rb[:], pb[:], xb[:]), waits=[mt, xtok] + rw)
            evac_prev[c % 2] = ev
            xbuf.release(c, ev)
            dst = yout if final else res_dram.ap()
            rtok = dma("sync", dst[c * 128:(c + 1) * 128, :], rb[:], waits=[ev])
            res_toks.append(rtok)
            consumers = [rtok]
            if stats_dst is not None:
                a_, b_ = raw_stats(rb[:], c, [ev])
                rstat_toks = [a_, b_]
                consumers += [a_, b_]
            rbuf.release(c, consumers)
        stats_out = None
        if stats_dst is not None:
            stats_out = stats_finalize(NT, stats_dst[0], stats_dst[1],
                                       rstat_toks)
        mem.free(Yt_sb, [fc_mt])
        for i, (h, _) in enumerate(ct_hs):
            mem.free(h, [fc_mt])
        guards = {"pA": evac_prev[0], "pB": evac_prev[1],
                  "pX": flush_toks[15]}
        return res_toks, stats_out, guards

    def transpose_k(cs, kc, kcw, waits):
        w = list(waits) + lastof([tr_state.get("pT")])
        ktr = t_op(lambda e, cs=cs: e.transpose(
            pT[:, 0:64], QKt_holder[0][:, cs], ident[0:64, 0:64]), waits=w)
        kev = s_op(lambda e, kc=kc: e.copy(kc[:, 0:64], pT[:, 0:64]),
                   waits=[ktr] + kcw)
        tr_state["pT"] = kev
        return ktr, kev

    QKt_holder = [None]
    def attention(final, st1tok, g1, x1st):
        P.phase = "AW"
        qkT_sb, gu = mem.alloc("qkT_sb", [128, KD * 128], BF16)
        t_qk = dma("gpsimd", qkT_sb[:].rearrange("p (k j) -> p k j", j=128),
                   qkT.rearrange("(k p) j -> p k j", p=128), waits=gu)
        vT_sb, gu = mem.alloc("vT_sb", [128, KD * D], BF16)
        t_v = dma("gpsimd", vT_sb[:].rearrange("p (k j) -> p k j", j=D),
                  vT.rearrange("(k p) j -> p k j", p=128), waits=gu)
        oT_sb, gu = mem.alloc("oT_sb", [128, KD * D], BF16)
        t_o = dma("gpsimd", oT_sb[:].rearrange("p (k j) -> p k j", j=D),
                  oT.rearrange("(k p) j -> p k j", p=128), waits=gu)
        lnw_sb, gu = mem.alloc("lnw_sb", [128, 2 * KD], F32)
        t_lnw = dma("scalar", lnw_sb[:, 0:KD], ln1w, waits=gu)
        t_lnw2 = dma("scalar", lnw_sb[:, KD:2 * KD], ln2w)
        lnbcol_bf, gu = mem.alloc("lnbcol_bf", [128, 2 * KD], BF16)
        t_lnb = dma("gpsimd", lnbcol_bf[:, 0:KD], ln1b_col, waits=gu)
        t_lnb2 = dma("gpsimd", lnbcol_bf[:, KD:2 * KD], ln2b_col)
        qkb_sb, gu = mem.alloc("qkb_sb", [1, 128], F32)
        t_qkb = dma("scalar", qkb_sb[:], qk_b, waits=gu)
        vb_sb, gu = mem.alloc("vb_sb", [1, D], F32)
        t_vb = dma("scalar", vb_sb[:], v_b, waits=gu)
        ob_bf, gu = mem.alloc("ob_bf", [1, D], BF16)
        t_ob = dma("gpsimd", ob_bf[:], o_b, waits=gu)
        qkb2c_sb, gu_q2 = mem.alloc("qkb2c_sb", [64, 2], F32)
        vb2_bf, gu_v2 = mem.alloc("vb2_bf", [1, D], BF16)
        pmask_sb, gu = mem.alloc("pmask_sb", [64, NCORES], F32)
        t_pm = dma("scalar", pmask_sb[:], pmask, waits=gu)

        cqk = None
        for k in range(KD):
            cqk = t_op(lambda e, k=k: e.matmul(
                pX[0:1, 0:128], lnbcol_bf[:, k:k + 1],
                qkT_sb[:, k * 128:(k + 1) * 128],
                start=(k == 0), stop=(k == KD - 1)),
                waits=[t_qk, t_lnb, g1["pX"]] if k == 0 else (),
                tok=(k == KD - 1))
        qkb2_tok = v_op(lambda e: e.tensor_tensor(
            qkb_sb[:], pX[0:1, 0:128], qkb_sb[:], ALU.add),
            waits=[cqk, t_qkb] + gu_q2, hazard=True)
        trq = t_op(lambda e: e.transpose(
            pX[0:64, 128:129], qkb_sb[0:1, 0:64], ones_f[0:1, 0:1]),
            waits=[qkb2_tok], tok=False)
        trq = t_op(lambda e: e.transpose(
            pX[0:64, 129:130], qkb_sb[0:1, 64:128], ones_f[0:1, 0:1]), tok=True)
        qkb2_tok = s_op(lambda e: e.copy(qkb2c_sb[:], pX[0:64, 128:130]),
                        waits=[trq], hazard=True)
        cv_ = None
        for h in range(2):
            for k in range(KD):
                cv_ = t_op(lambda e, k=k, h=h: e.matmul(
                    pX[0:1, h * 512:(h + 1) * 512], lnbcol_bf[:, k:k + 1],
                    vT_sb[:, k * D + h * 512: k * D + h * 512 + 512],
                    start=(k == 0), stop=(k == KD - 1)),
                    waits=([t_v, qkb2_tok] if (k == 0 and h == 0) else ()),
                    tok=(k == KD - 1))
        vb2_tok = v_op(lambda e: e.tensor_tensor(
            vb2_bf[:], pX[0:1, 0:1024], vb_sb[:], ALU.add),
            waits=[cv_, t_vb] + gu_v2)
        wscr_sb, gu_ws = mem.alloc("wscr_sb", [128, D], BF16)

        def scale_tiles(buf, nk, width, lncol0, waits):
            tok = None
            for k in range(nk):
                w0 = list(waits) if k == 0 else []
                if k == 0 and gu_ws:
                    w0 += gu_ws
                if tok is not None:
                    w0.append(tok)
                t1_ = v_op(lambda e, k=k: e.tensor_scalar_mul(
                    wscr_sb[:, 0:width], buf[:, k * width:(k + 1) * width],
                    lnw_sb[:, lncol0 + k: lncol0 + k + 1]), waits=w0,
                    hazard=True)
                tok = v_op(lambda e, k=k: e.tensor_copy(
                    buf[:, k * width:(k + 1) * width], wscr_sb[:, 0:width]),
                    waits=[t1_], hazard=True)
            return tok
        sc_qk = scale_tiles(qkT_sb, KD, 128, 0, [qkb2_tok, t_lnw])
        sc_v = scale_tiles(vT_sb, KD, D, 0, [vb2_tok, t_lnw, sc_qk])

        mem.free(wscr_sb, [sc_v])
        P.phase = "A2a"
        Qt_sb, gu_qt = mem.alloc("Qt_sb", [64, NT * 128], BF16)
        Kt_sb, gu_kt = mem.alloc("Kt_sb", [64, NT * 128], BF16)
        QKt_holder[0] = Kt_sb
        V_grp, gu_vg = mem.alloc("V_grp", [128, NT * D], BF16)
        xin_sb = Ring([mem.alloc(f"xin{i}", [128, D], F32)[0] for i in range(2)])
        xa_sb = Ring([mem.alloc(f"xa{i}", [128, D], BF16)[0] for i in range(2)])
        xaT_sb = Ring([mem.alloc(f"xaT{i}", [128, KD * 128], BF16)[0]
                       for i in range(2)])
        Kc_sb = Ring([mem.alloc(f"Kc{i}", [128, 64], BF16)[0] for i in range(2)])
        qk_scr, _ = mem.alloc("qk_scr", [64, 512], F32)
        ksc_sb, gu_ksc = mem.alloc("ksc_sb", [64, NT + 2], F32)
        ksc_bf, gu_kscb = mem.alloc("ksc_bf", [64, NT + 2], BF16)
        ksc0_tok = v_op(lambda e: e.memset(ksc_sb[:, 0:1], 0.0), waits=gu_ksc,
                        hazard=True)

        rs_toks = []
        for c in range(NT):
            xb, xw = xbuf.acquire(c)
            xtok = dma("scalar", xb[:], x1_dram.ap()[c * 128:(c + 1) * 128, :],
                       waits=xw + [x1st[c]])
            xi, xiw = xin_sb.acquire(c)
            itok = ln_apply(xi[:], xb[:], c, SC_SCALE1, SC_NMR1,
                            [xtok, st1tok] + xiw)
            xbuf.release(c, itok)
            a_, b_ = raw_stats(xi[:], c, [itok])
            xin_sb.release(c, [a_, b_])
            rs_toks = [a_, b_]
        stok = stats_finalize(NT, SC_SCALE, SC_NMR, rs_toks)
        t_ = v_op(lambda e: e.tensor_tensor(
            stats[:, SC_SC12 * NT: SC_SC12 * NT + NT],
            stats[:, SC_SCALE1 * NT: SC_SCALE1 * NT + NT],
            stats[:, SC_SCALE * NT: SC_SCALE * NT + NT], ALU.mult),
            waits=[stok, st1tok], hazard=True)
        t_ = v_op(lambda e: e.tensor_tensor(
            stats[:, SC_TMP * NT: SC_TMP * NT + NT],
            stats[:, SC_NMR1 * NT: SC_NMR1 * NT + NT],
            stats[:, SC_SCALE * NT: SC_SCALE * NT + NT], ALU.mult),
            waits=[t_], hazard=True)
        stok12 = v_op(lambda e: e.tensor_add(
            stats[:, SC_NM12 * NT: SC_NM12 * NT + NT],
            stats[:, SC_TMP * NT: SC_TMP * NT + NT],
            stats[:, SC_NMR * NT: SC_NMR * NT + NT]),
            waits=[t_], hazard=True)

        st_mm = None
        qk_done = []
        v_done = []
        qmt_last = None
        fe_out = {}
        ksc_prev = {"tok": None}

        def fe_a(c):
            xb, xw = xbuf.acquire(c)
            xtok = dma("scalar", xb[:], x1_dram.ap()[c * 128:(c + 1) * 128, :],
                       waits=xw + [x1st[c]])
            xa, xaw = xa_sb.acquire(c)
            atok = ln_apply(xa[:], xb[:], c, SC_SC12, SC_NM12,
                            [xtok, stok12] + xaw)
            xbuf.release(c, atok)
            xT, xTw = xaT_sb.acquire(c)
            trm, ev_xaT = transpose_tiles(
                lambda j, xa=xa: xa[:, j * 128:(j + 1) * 128],
                KD, pT, lambda b, n, xT=xT: xT[:, b * 128:(b + n) * 128],
                [atok, tok_const], evac_eng="scalar", dst_guard=xTw,
                banks=[pT, pC])
            xa_sb.release(c, trm)
            fe_out[c] = (xT, ev_xaT)
        qk_pend = {}
        el_pend = {}

        def be_a1(c):
            nonlocal qmt_last
            xT, ev_xaT = fe_out.pop(c)
            off = (c % 2) * 512
            w0 = [ev_xaT, qkb2_tok, sc_qk]
            if c < 2:
                w0.append(g1["pA"])
            if qk_done:
                w0.append(qk_done[-1])
            qmt = None
            for k in range(KD):
                qmt = t_op(lambda e, k=k, off=off, xT=xT: e.matmul(
                    pA[0:64, off:off + 128], qkT_sb[:, k * 128:k * 128 + 64],
                    xT[:, k * 128:(k + 1) * 128],
                    start=(k == 0), stop=(k == KD - 1)),
                    waits=w0 if k == 0 else (), tok=(k == KD - 1))
            for k in range(KD):
                qmt = t_op(lambda e, k=k, off=off, xT=xT: e.matmul(
                    pA[0:64, off + 128:off + 256],
                    qkT_sb[:, k * 128 + 64:(k + 1) * 128],
                    xT[:, k * 128:(k + 1) * 128],
                    start=(k == 0), stop=(k == KD - 1)), tok=(k == KD - 1))
            qmt_last = qmt
            qk_pend[c] = qmt
            w0 = [ev_xaT, vb2_tok, sc_v]
            if c == 0:
                w0.append(g1["pB"])
            if v_done:
                w0.append(v_done[-1])
            t_op(lambda e: e.matmul(pB[:, 0:512], ones_bf[:], vb2_bf[:, 0:512],
                                    start=True, stop=False), waits=w0, tok=False)
            t_op(lambda e: e.matmul(pB[:, 512:1024], ones_bf[:],
                                    vb2_bf[:, 512:1024], start=True, stop=False),
                 tok=False)
            vmt = None
            for k in range(KD):
                for h in range(2):
                    vmt = t_op(lambda e, k=k, h=h, xT=xT: e.matmul(
                        pB[:, h * 512:(h + 1) * 512],
                        xT[:, k * 128:(k + 1) * 128],
                        vT_sb[:, k * D + h * 512: k * D + h * 512 + 512],
                        start=False, stop=(k == KD - 1)),
                        tok=(k == KD - 1 and h == 1))
            xaT_sb.release(c, vmt)
            vev = s_op(lambda e, c=c: e.copy(
                V_grp[:, c * D:(c + 1) * D], pB[:]),
                waits=[vmt] + (gu_vg if c == 0 else []), hazard=True)
            v_done.append(vev)

        def be_a2(c):
            cs = slice(c * 128, (c + 1) * 128)
            off = (c % 2) * 512
            qmt = qk_pend.pop(c)
            e1 = v_op(lambda e, off=off: e.tensor_scalar(
                qk_scr[0:64, 0:128], pA[0:64, off:off + 128],
                qkb2c_sb[:, 0:1], 0.0, ALU.add, ALU.min), waits=[qmt])
            e1 = v_op(lambda e, off=off: e.tensor_scalar(
                qk_scr[0:64, 128:256], pA[0:64, off + 128:off + 256],
                qkb2c_sb[:, 1:2], 0.0, ALU.add, ALU.min), waits=[qmt])
            e2 = v_op(lambda e, off=off: e.tensor_scalar(
                qk_scr[0:64, 256:384], pA[0:64, off:off + 128],
                qkb2c_sb[:, 0:1], 0.0, ALU.add, ALU.max), waits=[qmt])
            e2 = v_op(lambda e, off=off: e.tensor_scalar(
                qk_scr[0:64, 384:512], pA[0:64, off + 128:off + 256],
                qkb2c_sb[:, 1:2], 0.0, ALU.add, ALU.max), waits=[qmt])
            e3 = s_op(lambda e: e.activation(
                out=qk_scr[0:64, 0:256], in_=qk_scr[0:64, 0:256], func=AF.Exp),
                waits=[e1], hazard=True)
            e4a = v_op(lambda e, cs=cs: e.tensor_tensor(
                Qt_sb[:, cs], qk_scr[0:64, 0:128], qk_scr[0:64, 256:384],
                ALU.add), waits=[e2, e3] + (gu_qt if c == 0 else []),
                hazard=True)
            e4 = v_op(lambda e, cs=cs: e.tensor_tensor(
                Kt_sb[:, cs], qk_scr[0:64, 128:256], qk_scr[0:64, 384:512],
                ALU.add), waits=[] + (gu_kt if c == 0 else []), hazard=True)
            qk_done.append(e4)
            el_pend[c] = e4
            kst = v_op(lambda e, cs=cs, c=c: e.reduce_sum(
                ksc_sb[:, NT + 1:NT + 2], Kt_sb[:, cs], AX),
                waits=[e4] + (gu_ksc if c == 0 else []) +
                ([ksc_prev["tok"]] if ksc_prev["tok"] else []), hazard=True)
            ka = v_op(lambda e, c=c: e.tensor_add(
                ksc_sb[:, c + 1:c + 2], ksc_sb[:, NT + 1:NT + 2],
                ksc_sb[:, c:c + 1]),
                waits=[kst, ksc0_tok], hazard=True)
            ksc_prev["tok"] = ka

        def be_a3(c):
            nonlocal st_mm
            cs = slice(c * 128, (c + 1) * 128)
            e4 = el_pend.pop(c)
            kc, kcw = Kc_sb.acquire(c)
            ktr, kev = transpose_k(cs, kc, kcw, [e4])
            stw = [kev, v_done[c]]
            if c == 0:
                stw.append(vb2_tok)
            if st_add[0] is not None:
                stw.append(st_add[0])
            for h in range(2):
                st_mm = t_op(lambda e, c=c, h=h, kc=kc: e.matmul(
                    pX[0:64, h * 512:(h + 1) * 512], kc[:, 0:64],
                    V_grp[:, c * D + h * 512: c * D + h * 512 + 512],
                    start=True, stop=True),
                    waits=stw if h == 0 else (), tok=(h == 1))
            Kc_sb.release(c, st_mm)
            if c == 0:
                st_add[0] = v_op(lambda e: e.tensor_copy(
                    kvloc_sb[:], pX[0:64, :]), waits=[st_mm] + gu_kvl,
                    hazard=True)
            else:
                st_add[0] = v_op(lambda e: e.tensor_add(
                    kvloc_sb[:], kvloc_sb[:], pX[0:64, :]),
                    waits=[st_mm, st_add[0]], hazard=True)
            if c < NT - 1:
                kb, kbw = kvb_sb.acquire(c)
                kvev = v_op(lambda e, kb=kb: e.tensor_copy(kb[:], kvloc_sb[:]),
                            waits=[st_add[0]] + kbw, hazard=True)
                kv_sp[c % 2] = kvev
                sd = dma("sync", kv_dram.ap()[c], kb[:], waits=[kvev])
                kv_st[c] = sd
                kvb_sb.release(c, sd)

        kvb_sb = Ring([mem.alloc(f"kvb{i}", [64, D], BF16)[0] for i in range(2)])
        kvloc_sb, gu_kvl = mem.alloc("kvloc_sb", [64, D], F32)
        st_add = [None]
        kv_sp = [None, None]
        kv_st = {}
        n_fe = 0
        for c in range(NT + 2):
            while n_fe <= c + 1 and n_fe < NT:
                fe_a(n_fe)
                n_fe += 1
            if c < NT:
                be_a1(c)
            if 0 <= c - 1 < NT:
                be_a2(c - 1)
            if 0 <= c - 2 < NT:
                be_a3(c - 2)

        P.phase = "AG"
        mem.free(qkT_sb, [qmt_last])
        mem.free(vT_sb, [v_done[-1]])
        mem.free(qk_scr, [qk_done[-1]])
        for r_ in (xa_sb, xaT_sb, xin_sb):
            for i_ in range(r_.n):
                mem.free(r_.bufs[i_], r_.free[i_])
        st_sb, gu_st = mem.alloc("st_sb", [64, 1026], F32)
        pad_tok = v_op(lambda e: e.memset(st_sb[:, 1024:1026], 0.0),
                       waits=gu_st)
        ks_tok = v_op(lambda e: e.tensor_copy(
            st_sb[:, 1024:1025], ksc_sb[:, NT:NT + 1]),
            waits=[ksc_prev["tok"], pad_tok], hazard=True)
        st_ev = v_op(lambda e: e.tensor_copy(st_sb[:, 0:1024], kvloc_sb[:]),
                     waits=[st_add[0], pad_tok], hazard=True)
        t_stag = dma("sync", st_ag_in.ap(), st_sb[:],
                     waits=[ks_tok, st_ev, pad_tok])
        tok_stag = coll("AllGather", ALU.bypass, [st_ag_in.ap().opt()],
                        [st_ag_out.ap().opt()], [t_stag])
        st_all, gu_sa = mem.alloc("st_all", [64, NCORES * 1026], F32)
        t_sall = dma("sync", st_all[:].rearrange("p (r n) -> p r n", n=1026),
                     st_ag_out.ap().rearrange("(r p) n -> p r n", p=64),
                     waits=[tok_stag] + gu_sa)
        ksc2_sb, gu_ksc2 = mem.alloc("ksc2_sb", [64, NT + 2], F32)
        KV_run, gu_kv = mem.alloc("KV_run", [64, 1026], F32)
        KVp_bf, gu_kb = mem.alloc("KVp_bf", [64, 1024], BF16)
        pfx = v_op(lambda e: e.tensor_scalar_mul(
            KV_run[:], st_all[:, 0:1026], pmask_sb[:, 0:1]),
            waits=[t_sall, t_pm] + gu_kv)
        for r in range(1, NCORES):
            pfx = v_op(lambda e, r=r: e.scalar_tensor_tensor(
                KV_run[:], st_all[:, r * 1026:(r + 1) * 1026],
                pmask_sb[:, r:r + 1], KV_run[:], ALU.mult, ALU.add),
                waits=[pfx], hazard=True)
        kvb_tok = v_op(lambda e: e.tensor_copy(KVp_bf[:], KV_run[:, 0:1024]),
                       waits=[pfx] + gu_kb, hazard=True)
        ksfix = v_op(lambda e: e.tensor_scalar_add(
            ksc_sb[:, NT + 2:2 * NT + 3] if False else ksc2_sb[:, 0:NT + 1],
            ksc_sb[:, 0:NT + 1], KV_run[:, 1024:1025]),
            waits=[pfx, ksc_prev["tok"]] + gu_ksc2, hazard=True)
        ksfix = v_op(lambda e: e.tensor_copy(ksc_bf[:, 0:NT + 1],
                                             ksc2_sb[:, 0:NT + 1]),
                     waits=[ksfix] + gu_kscb, hazard=True)
        mem.free(st_all, [pfx])
        mem.free(st_sb, [t_stag])

        P.phase = "A2c"
        Amask_sb = Ring([mem.alloc(f"Am{i}", [128, 128], BF16)[0]
                         for i in range(2)])
        attnT_sb = Ring([mem.alloc(f"at{i}", [128, KD * 128], BF16)[0]
                         for i in range(2)])
        xmid_sb = Ring([mem.alloc(f"xm{i}", [128, D], F32)[0] for i in range(2)])
        den_sb, _ = mem.alloc("den_sb", [1, 256], F32)
        denrep_sb = Ring([mem.alloc(f"dr{i}", [128, 128], F32)[0]
                          for i in range(2)])
        kvf_sb = Ring([mem.alloc(f"kvf{i}", [64, D], BF16)[0] for i in range(3)])

        kv_ld = {}

        def load_state(c):
            kb, kbw = kvf_sb.acquire(c)
            if c == 0:
                tok = v_op(lambda e, kb=kb: e.tensor_copy(kb[:], KVp_bf[:]),
                           waits=[kvb_tok] + kbw)
            else:
                ld = dma("sync", kb[:], kv_dram.ap()[c - 1],
                         waits=kbw + [kv_st[c - 1]])
                tok = v_op(lambda e, kb=kb: e.tensor_add(kb[:], kb[:],
                                                         KVp_bf[:]),
                           waits=[ld, kvb_tok])
            kv_ld[c] = (kb, tok)

        div_prev = None
        div_hist = []
        drev_hist = []
        o_guard = st_ev
        xmtok = None
        xm_dma = []
        load_state(0)
        load_state(1)
        for c in range(NT):
            cs = slice(c * 128, (c + 1) * 128)
            off = (c % 2) * 512
            am, amw = Amask_sb.acquire(c)
            aw0 = [qk_done[-1]]
            if c >= 2:
                aw0 += [div_hist[c - 2], drev_hist[c - 2]]
            atm = t_op(lambda e, cs=cs, off=off: e.matmul(
                pA[:, off:off + 128], Kt_sb[:, cs], Qt_sb[:, cs],
                start=True, stop=True), waits=aw0, tok=True)
            amtok = v_op(lambda e, am=am, off=off: e.tensor_tensor(
                am[:], pA[:, off:off + 128], maskT[:], ALU.mult),
                waits=[atm] + amw)
            t_op(lambda e, am=am, off=off: e.matmul(
                pA[0:1, off + 128:off + 256], onecol_bf[:], am[:],
                start=True, stop=False), waits=[amtok], tok=False)
            dmm = t_op(lambda e, cs=cs, off=off, c=c: e.matmul(
                pA[0:1, off + 128:off + 256], ksc_bf[:, c:c + 1],
                Qt_sb[:, cs], start=False, stop=True), waits=[ksfix], tok=True)
            de1 = v_op(lambda e, off=off: e.tensor_scalar_add(
                den_sb[:, 0:128], pA[0:1, off + 128:off + 256], ATTN_EPS),
                waits=[dmm])
            de2 = v_op(lambda e: e.reciprocal(den_sb[:, 128:256],
                                              den_sb[:, 0:128]),
                       waits=[de1], hazard=True)
            drm = t_op(lambda e, off=off: e.matmul(
                pA[:, off + 256:off + 384], ones_f[:], den_sb[:, 128:256],
                start=True, stop=True), waits=[de2], tok=True)
            dr, drw = denrep_sb.acquire(c)
            drev = s_op(lambda e, dr=dr, off=off: e.copy(
                dr[:], pA[:, off + 256:off + 384]), waits=[drm] + drw)
            drev_hist.append(drev)
            kvbuf, kvtok = kv_ld.pop(c)
            nmt = None
            for k in range(KD):
                t_op(lambda e, k=k, c=c, am=am: e.matmul(
                    pB[:, k * 128:(k + 1) * 128],
                    V_grp[:, c * D + k * 128: c * D + k * 128 + 128], am[:],
                    start=True, stop=False),
                    waits=(([amtok, kvtok, v_done[-1]] +
                            ([div_prev] if div_prev else []))
                           if k == 0 else ()), tok=False)
                nmt = t_op(lambda e, k=k, cs=cs, kvbuf=kvbuf: e.matmul(
                    pB[:, k * 128:(k + 1) * 128],
                    kvbuf[:, k * 128:(k + 1) * 128],
                    Qt_sb[:, cs], start=False, stop=True),
                    tok=(k == KD - 1))
            Amask_sb.release(c, nmt)
            if c + 2 < NT:
                load_state(c + 2)
            kvf_sb.release(c, nmt)
            at, atw = attnT_sb.acquire(c)
            div_prev = v_op(lambda e, at=at, dr=dr: e.tensor_tensor(
                at[:].rearrange("p (k t) -> p k t", t=128),
                pB[:].rearrange("p (k t) -> p k t", t=128),
                dr[:].unsqueeze(1).broadcast_to([128, KD, 128]), ALU.mult),
                waits=[nmt, drev] + atw)
            denrep_sb.release(c, div_prev)
            div_hist.append(div_prev)
            w0 = [div_prev, t_ob, t_o, o_guard]
            t_op(lambda e: e.matmul(pX[:, 0:512], ones_bf[:], ob_bf[:, 0:512],
                                    start=True, stop=False), waits=w0, tok=False)
            t_op(lambda e: e.matmul(pX[:, 512:1024], ones_bf[:],
                                    ob_bf[:, 512:1024], start=True, stop=False),
                 tok=False)
            omt = None
            for k in range(KD):
                for h in range(2):
                    omt = t_op(lambda e, k=k, h=h, at=at: e.matmul(
                        pX[:, h * 512:(h + 1) * 512],
                        at[:, k * 128:(k + 1) * 128],
                        oT_sb[:, k * D + h * 512: k * D + h * 512 + 512],
                        start=False, stop=(k == KD - 1)),
                        tok=(k == KD - 1 and h == 1))
            attnT_sb.release(c, omt)
            rb, rw = rbuf.acquire(c)
            xtok = dma("scalar", rb[:], x1_dram.ap()[c * 128:(c + 1) * 128, :],
                       waits=rw + [x1st[c]])
            xi = sq_scr
            iw = [xtok, st1tok]
            if xmtok is not None:
                iw.append(xmtok)
            if sq_chain[0] is not None:
                iw.append(sq_chain[0])
            itok = ln_apply(xi[:], rb[:], c, SC_SCALE1, SC_NMR1, iw)
            sq_chain[0] = itok
            rbuf.release(c, itok)
            xm, xmw = xmid_sb.acquire(c)
            xmtok = v_op(lambda e, xm=xm: e.tensor_add(
                xm[:], pX[:], sq_scr[:]), waits=[omt, itok] + xmw)
            o_guard = xmtok
            xm_dma.append(dma("sync", xmid_dram.ap()[c * 128:(c + 1) * 128, :],
                              xm[:], waits=[xmtok]))
            xmid_sb.release(c, xm_dma[-1])
        xmtok_last = xmtok
        dmm_last = dmm
        nmt_last = nmt
        omt_last = omt
        mem.free(Qt_sb, [dmm_last, nmt_last])
        mem.free(Kt_sb, [dmm_last, nmt_last])
        mem.free(V_grp, [nmt_last])
        mem.free(oT_sb, [omt_last])
        for r_ in (Kc_sb, Amask_sb, attnT_sb, xmid_sb, denrep_sb, kvf_sb,
                   kvb_sb):
            for i_ in range(r_.n):
                mem.free(r_.bufs[i_], r_.free[i_])
        for h_ in (qkb_sb, vb_sb, ob_bf, qkb2c_sb, vb2_bf,
                   pmask_sb, den_sb, KV_run, KVp_bf, ksc_sb, ksc_bf,
                   ksc2_sb, kvloc_sb):
            mem.free(h_, [omt_last, dmm_last, xmtok_last])
        mem.free(maskT, [div_prev])

        P.phase = "A3w"
        f1T_sb = []
        t_f1s = []
        for k in range(KD):
            h_, gu = mem.alloc(f"f1T_sb{k}", [128, F1], BF16)
            f1T_sb.append(h_)
            t_f1s.append(dma("gpsimd", h_[:], f1T[k * 128:(k + 1) * 128, :],
                             waits=gu))
        t_f1l = lastof(t_f1s)
        f1bc_sb, gu = mem.alloc("f1bc_sb", [128, 32], F32)
        t_f1b = dma("scalar", f1bc_sb[:], f1b_col, waits=gu)
        f1b2_sb, gu_f1b2 = mem.alloc("f1b2_sb", [128, 32], F32)
        cf1 = None
        cf1v = None
        for j in range(32):
            for k in range(KD):
                cf1 = t_op(lambda e, j=j, k=k: e.matmul(
                    pA[:, 0:1],
                    f1T_sb[k][:, j * 128:(j + 1) * 128],
                    lnbcol_bf[:, KD + k: KD + k + 1],
                    start=(k == 0), stop=(k == KD - 1)),
                    waits=((t_f1l + [t_lnb2, div_prev, drev_hist[-1]])
                           if (j == 0 and k == 0) else
                           ([cf1v] if k == 0 else ())),
                    tok=(k == KD - 1))
            cf1v = v_op(lambda e, j=j: e.tensor_copy(
                f1b2_sb[:, j:j + 1], pA[:, 0:1]),
                waits=[cf1] + (gu_f1b2 if j == 0 else []), hazard=True)
        f1b2_tok = v_op(lambda e: e.tensor_add(
            f1b2_sb[:], f1b2_sb[:], f1bc_sb[:]),
            waits=[cf1v, t_f1b], hazard=True)
        f1scr_sb, gu_f1s = mem.alloc("f1scr_sb", [128, F1], BF16)
        sc_f1 = None
        for k in range(KD):
            w0 = ([f1b2_tok, t_lnw2] + gu_f1s) if k == 0 else []
            if sc_f1 is not None:
                w0.append(sc_f1)
            t1_ = v_op(lambda e, k=k: e.tensor_scalar_mul(
                f1scr_sb[:], f1T_sb[k][:], lnw_sb[:, KD + k: KD + k + 1]),
                waits=w0, hazard=True)
            sc_f1 = v_op(lambda e, k=k: e.tensor_copy(f1T_sb[k][:],
                                                      f1scr_sb[:]),
                         waits=[t1_], hazard=True)
        mem.free(f1scr_sb, [sc_f1])
        mem.free(lnbcol_bf, [cf1])
        mem.free(lnw_sb, [sc_f1])

        rs_toks = []
        for c in range(NT):
            xb, xw = xbuf.acquire(c)
            xtok = dma("sync", xb[:], xmid_dram.ap()[c * 128:(c + 1) * 128, :],
                       waits=xw + [xm_dma[c]])
            a_, b_ = raw_stats(xb[:], c, [xtok, xmtok_last])
            xbuf.release(c, [a_, b_])
            rs_toks = [a_, b_]
        stok2 = stats_finalize(NT, SC_SCALE, SC_NMR, rs_toks)

        P.phase = "A3"
        f2T_sb = []
        t_f2s = []
        for k in range(32):
            h_, gu = mem.alloc(f"f2T_sb{k}", [128, D], BF16)
            f2T_sb.append(h_)
            t_f2s.append(dma("gpsimd", h_[:], f2T[k * 128:(k + 1) * 128, :],
                             waits=gu))
        t_f2l = lastof(t_f2s)
        f2b_bf, gu = mem.alloc("f2b_bf", [1, D], BF16)
        t_f2b = dma("gpsimd", f2b_bf[:], f2_b, waits=gu)
        xh_sb = Ring([mem.alloc(f"xh{i}", [128, D], BF16)[0] for i in range(2)])
        xhg_sb = Ring([mem.alloc(f"xhg{i}", [128, KD * 512], BF16)[0]
                       for i in range(2)])
        g_hs = [mem.alloc(f"g_grp{i}", [128, 16 * 512], BF16) for i in range(2)]
        g_grp2 = [h for h, _ in g_hs]
        gu_g = [g for _, gu in g_hs for g in gu]

        res_toks = []
        rstat_toks = []
        NG3 = 8
        fe3 = {}

        def fe_a3(g_i):
            xg, xgw = xhg_sb.acquire(g_i)
            evs = []
            for tt in range(4):
                c = g_i * 4 + tt
                xb, xw = xbuf.acquire(c)
                xtok = dma("sync", xb[:],
                           xmid_dram.ap()[c * 128:(c + 1) * 128, :],
                           waits=xw + [xm_dma[c]])
                xh, xhw = xh_sb.acquire(c)
                htok = v_op(lambda e, xh=xh, xb=xb, c=c: e.tensor_scalar(
                    xh[:], xb[:], stats[:, SC_SCALE * NT + c: SC_SCALE * NT + c + 1],
                    stats[:, SC_NMR * NT + c: SC_NMR * NT + c + 1],
                    ALU.mult, ALU.add),
                    waits=[xtok, stok2] + xhw, hazard=True)
                xbuf.release(c, htok)
                pb3 = pT if tt % 2 == 0 else pC
                w = [htok] + lastof([tr_state.get(pb3.name)])
                if tt == 0:
                    w += xgw
                mt3 = None
                for j in range(KD):
                    mt3 = t_op(lambda e, j=j, xh=xh, pb3=pb3: e.transpose(
                        pb3[:, j * 128:(j + 1) * 128],
                        xh[:, j * 128:(j + 1) * 128], ident[:]),
                        waits=w if j == 0 else (), tok=(j == KD - 1))
                dstv = xg[:].rearrange("p (k g) -> p k g", g=512)[
                    :, :, tt * 128:(tt + 1) * 128]
                ev = s_op(lambda e, dstv=dstv, pb3=pb3: e.copy(
                    dstv, pb3[:].rearrange("p (k t) -> p k t", t=128)),
                    waits=[mt3])
                tr_state[pb3.name] = ev
                xh_sb.release(c, ev)
                evs.append(ev)
            fe3[g_i] = (xg, evs[-1])

        f2_prev = [div_prev, xmtok_last]
        gel2 = [cf1v, None]
        fmt_pg = None
        gel = None
        fm = None
        fmt = None
        for g_i in range(NG3):
            if g_i == 0:
                fe_a3(0)
                fe_a3(1)
            xg, xgtok = fe3.pop(g_i)
            for j in range(32):
                offj = (j % 2) * 512
                w0 = [xgtok, sc_f1]
                if gel2[j % 2] is not None:
                    w0.append(gel2[j % 2])
                if fmt_pg is not None and j == 0:
                    w0.append(fmt_pg)
                for k in range(KD):
                    fm = t_op(lambda e, j=j, k=k, offj=offj, xg=xg: e.matmul(
                        pA[:, offj:offj + 512],
                        f1T_sb[k][:, j * 128:(j + 1) * 128],
                        xg[:, k * 512:(k + 1) * 512],
                        start=(k == 0), stop=(k == KD - 1)),
                        waits=w0 if k == 0 else (), tok=(k == KD - 1))
                gw = [fm, f1b2_tok]
                if g_i == 0 and j < 2:
                    gw += gu_g
                if fmt_pg is not None:
                    gw.append(fmt_pg)
                gel = s_op(lambda e, j=j, offj=offj: e.activation(
                    out=g_grp2[j // 16][:, (j % 16) * 512:(j % 16 + 1) * 512],
                    in_=pA[:, offj:offj + 512],
                    func=AF.Gelu, bias=f1b2_sb[:, j:j + 1]), waits=gw)
                gel2[j % 2] = gel
            xhg_sb.release(g_i, fm)
            if g_i + 2 < NG3:
                fe_a3(g_i + 2)
            for tt in range(4):
                c = g_i * 4 + tt
                pb2 = pB if tt % 2 == 0 else pX
                w0 = [gel, t_f2b] + t_f2l
                if f2_prev[tt % 2] is not None:
                    w0.append(f2_prev[tt % 2])
                t_op(lambda e, pb2=pb2: e.matmul(
                    pb2[:, 0:512], ones_bf[:], f2b_bf[:, 0:512],
                    start=True, stop=False), waits=w0, tok=False)
                t_op(lambda e, pb2=pb2: e.matmul(
                    pb2[:, 512:1024], ones_bf[:], f2b_bf[:, 512:1024],
                    start=True, stop=False), tok=False)
                fmt = None
                for f in range(32):
                    for h in range(2):
                        fmt = t_op(lambda e, f=f, h=h, tt=tt, pb2=pb2: e.matmul(
                            pb2[:, h * 512:(h + 1) * 512],
                            g_grp2[f // 16][:, (f % 16) * 512 + tt * 128:
                                            (f % 16) * 512 + tt * 128 + 128],
                            f2T_sb[f][:, h * 512: h * 512 + 512],
                            start=False, stop=(f == 31)),
                            tok=(f == 31 and h == 1))
                xb, xw = xbuf.acquire(c)
                xmtok2 = dma("scalar", xb[:],
                             xmid_dram.ap()[c * 128:(c + 1) * 128, :],
                             waits=xw + [xm_dma[c]])
                rb, rw = rbuf.acquire(c)
                x1tok = dma("scalar", rb[:], x1_dram.ap()[c * 128:(c + 1) * 128, :],
                            waits=rw + [x1st[c]])
                ev1 = v_op(lambda e, pb2=pb2, xb=xb: e.tensor_add(
                    xb[:], pb2[:], xb[:]), waits=[fmt, xmtok2])
                ev2 = v_op(lambda e, rb=rb, xb=xb: e.tensor_add(
                    xb[:], xb[:], rb[:]), waits=[ev1, x1tok])
                f2_prev[tt % 2] = ev1
                rbuf.release(c, ev2)
                dst = yout if final else x2_dram.ap()
                rtok = dma("sync", dst[c * 128:(c + 1) * 128, :], xb[:],
                           waits=[ev2])
                res_toks.append(rtok)
                consumers = [rtok]
                if not final:
                    a_, b_ = raw_stats(xb[:], c, [ev2])
                    rstat_toks = [a_, b_]
                    consumers += [a_, b_]
                xbuf.release(c, consumers)
            fmt_pg = fmt
        stats_out = None
        if not final:
            stats_out = stats_finalize(NT, SC_SCALE2, SC_NMR2, rstat_toks)
        for h_ in f1T_sb:
            mem.free(h_, [fm])
        for h_ in f2T_sb:
            mem.free(h_, [fmt])
        mem.free(f2b_bf, [fmt])
        mem.free(f1b2_sb, [gel])
        mem.free(f1bc_sb, [f1b2_tok])
        for h_, _ in g_hs:
            mem.free(h_, [fmt])
        for r_ in (xhg_sb, xh_sb):
            for i_ in range(r_.n):
                mem.free(r_.bufs[i_], r_.free[i_])
        guards = {"pA": gel, "pB": f2_prev[0], "pX": f2_prev[1]}
        return res_toks, stats_out, guards

    all_out = []
    if STAGE == 1:
        res1, st1, g1 = ffn(0, x0, None, None, None, True)
        all_out += res1
    elif STAGE == 2:
        res1, st1, g1 = ffn(0, x0, x1_dram, None, (SC_SCALE1, SC_NMR1), False)
        res2, st2x, g2 = attention(True, st1, g1, res1)
        all_out += res2
    else:
        res1, st1, g1 = ffn(0, x0, x1_dram, None, (SC_SCALE1, SC_NMR1), False)
        res2, st2x, g2 = attention(False, st1, g1, res1)
        res3, _, _ = ffn(1, x2_dram.ap(), None, (SC_SCALE2, SC_NMR2), None,
                         True, stats_ready=st2x, pg=g2, xsrc=res2)
        all_out += res3

    P.emit("sync", lambda e: e.nop(), waits=lastof(all_out))

    P.realize_values()
    runners = {e: P.runner(e) for e in ENGS}

    with es:
        with nc.Block() as block:
            block.tensor(lambda e: runners["tensor"](e))
            block.vector(lambda e: runners["vector"](e))
            block.scalar(lambda e: runners["scalar"](e))
            block.sync(lambda e: runners["sync"](e))
            block.gpsimd(lambda e: runners["gpsimd"](e))
    return nc


def prep_inputs(inputs):
    gi = lambda k: np.asarray(inputs[k])
    x = gi("L0_attn_x")
    cc = np.ascontiguousarray
    phiT = cc(gi("L0_ffn_phi_w").T)
    phi_b = cc(gi("L0_ffn_phi_b")[None, :])
    qkT = cc(np.concatenate([gi("L0_attn_q_w"), gi("L0_attn_k_w")], axis=0).T)
    qk_b = cc(np.concatenate([gi("L0_attn_q_b"), gi("L0_attn_k_b")])[None, :])
    vT = cc(gi("L0_attn_v_w").T)
    v_b = cc(gi("L0_attn_v_b")[None, :])
    oT = cc(gi("L0_attn_o_w").T)
    o_b = cc(gi("L0_attn_o_b")[None, :])
    f1T = cc(gi("L0_attn_f1_w").T)
    f1_b = cc(gi("L0_attn_f1_b")[None, :])
    f2T = cc(gi("L0_attn_f2_w").T)
    f2_b = cc(gi("L0_attn_f2_b")[None, :])
    ln1w = cc(gi("L0_attn_ln1_w").reshape(KD, 128).T)
    ln1b = cc(gi("L0_attn_ln1_b")[None, :])
    ln2w = cc(gi("L0_attn_ln2_w").reshape(KD, 128).T)
    ln2b = cc(gi("L0_attn_ln2_b")[None, :])
    ew = gi("L0_ffn_expert_w")
    eb = gi("L0_ffn_expert_b")
    ln1b_c = cc(gi("L0_attn_ln1_b").reshape(KD, 128).T)
    ln2b_c = cc(gi("L0_attn_ln2_b").reshape(KD, 128).T)
    f1b_c = cc(gi("L0_attn_f1_b").reshape(32, 128).T)
    maps = []
    for c in range(NCORES):
        b, s = c // 4, c % 4
        pm = np.zeros((64, NCORES), dtype=np.float32)
        pm[:, 4 * b: c] = 1.0
        maps.append({
            "x0": cc(x[b, s * T:(s + 1) * T, :]),
            "phiT": phiT, "phi_b": phi_b, "qkT": qkT, "qk_b": qk_b,
            "vT": vT, "v_b": v_b, "oT": oT, "o_b": o_b,
            "f1T": f1T, "f1_b": f1_b, "f2T": f2T, "f2_b": f2_b,
            "ln1w": ln1w, "ln1b": ln1b, "ln2w": ln2w, "ln2b": ln2b,
            "ew": cc(ew[c * EPC:(c + 1) * EPC]),
            "eb": cc(eb[c * EPC:(c + 1) * EPC].reshape(1, -1)),
            "ln1b_col": ln1b_c, "ln2b_col": ln2b_c, "f1b_col": f1b_c,
            "pmask": pm,
        })
    return maps


def kernel(**inputs):
    maps = prep_inputs(inputs)
    nc = build_program()
    res = run_bass_kernel_spmd(nc, maps, core_ids=list(range(NCORES)))
    outs = [res.results[c]["y"] for c in range(NCORES)]
    return np.stack([np.concatenate([outs[b * 4 + s] for s in range(4)], axis=0)
                     for b in range(2)], axis=0)
